# version 01
# baseline (speedup 1.0000x reference)
import numpy as np

import concourse.bass as bass
import concourse.mybir as mybir
from concourse.bass_utils import run_bass_kernel_spmd

F32 = mybir.dt.float32
BF16 = mybir.dt.bfloat16
AF = mybir.ActivationFunctionType
ALU = mybir.AluOpType
AX = mybir.AxisListType

D = 1024
DI = 2048
NH = 8
HD = 256
LCORE = 4096
C = 256
Q = 128
NCH_FULL = LCORE // C
KD = D // 128
KI = DI // 128
KH = 4096 // 128
NFP = 50
FW = NFP * 128
LN_EPS = 1e-5
QEPS = 1e-6
MSK_NEG = -60000.0
CP3 = C + 3


def _host_prep(inputs):
    f32 = np.float32
    u = np.asarray(inputs["L0_attn_u"], f32)
    W_in = np.asarray(inputs["L0_attn_W_in"], f32)
    conv_w = np.asarray(inputs["L0_attn_conv_w"], f32)
    conv_b = np.asarray(inputs["L0_attn_conv_b"], f32)
    dt_bias = np.asarray(inputs["L0_attn_dt_bias"], f32)
    A_log = np.asarray(inputs["L0_attn_A_log"], f32)
    D_param = np.asarray(inputs["L0_attn_D_param"], f32)
    ln_gamma = np.asarray(inputs["L0_attn_ln_gamma"], f32)
    ln_beta = np.asarray(inputs["L0_attn_ln_beta"], f32)
    W_gate = np.asarray(inputs["L0_attn_W_gate"], f32)
    b_gate = np.asarray(inputs["L0_attn_b_gate"], f32)
    W_out = np.asarray(inputs["L0_attn_W_out"], f32)
    W_fc = np.asarray(inputs["L0_ffn_W_fc"], f32)
    ln_w = np.asarray(inputs["L0_ffn_ln_w"], f32)
    ln_b = np.asarray(inputs["L0_ffn_ln_b"], f32)
    W_proj = np.asarray(inputs["L0_ffn_W_proj"], f32)

    c48 = np.zeros((128, D), f32)
    c48[0:16] = W_in[8192:8208]
    c48[64:72] = W_in[8224:8232]
    c49 = np.zeros((128, D), f32)
    c49[0:16] = W_in[8208:8224]
    c49[32:48] = W_in[8208:8224]
    Wp = np.concatenate([W_in[0:2048], W_in[2048:4096], W_in[6144:8192], c48, c49], axis=0)
    WpT = np.ascontiguousarray(Wp.T).reshape(KD, 128, FW)

    def blkT(W):
        K = W.shape[1]
        return np.ascontiguousarray(W.T).reshape(K // 128, 128, W.shape[0])

    cw = conv_w[:, 0, :]
    cwp = np.zeros((18, 128, 4), f32)
    cwp[0:16] = cw[0:2048].reshape(16, 128, 4)
    cwp[16, 0:16] = cw[2048:2064]
    cwp[17, 0:16] = cw[2064:2080]
    cwp[17, 32:48] = cw[2064:2080]
    cbp = np.zeros((18, 128), f32)
    cbp[0:16] = conv_b[0:2048].reshape(16, 128)
    cbp[16, 0:16] = conv_b[2048:2064]
    cbp[17, 0:16] = conv_b[2064:2080]
    cbp[17, 32:48] = conv_b[2064:2080]

    tri = np.fromfunction(lambda i, j: np.where(i <= j, 0.0, MSK_NEG), (128, 128)).astype(f32)
    sel = np.zeros((8, 8 * 128), f32)
    for h in range(8):
        sel[h, h * 128:(h + 1) * 128] = 1.0
    selpair = np.zeros((8, 4 * 64), f32)
    for i in range(4):
        selpair[2 * i, i * 64 + 0:i * 64 + 16] = 1.0
        selpair[2 * i + 1, i * 64 + 32:i * 64 + 48] = 1.0

    shared = dict(
        w_in=WpT,
        w_gate=blkT(W_gate), w_out=blkT(W_out),
        w_fc=blkT(W_fc), w_proj=blkT(W_proj),
        convw=np.ascontiguousarray(cwp.transpose(1, 0, 2)).reshape(128, 72),
        convb=np.ascontiguousarray(cbp.T),
        dcol=np.ascontiguousarray(np.repeat(D_param, HD).reshape(KI, 128).T),
        gam=np.ascontiguousarray(ln_gamma.reshape(KI, 128).T),
        bet=np.ascontiguousarray(ln_beta.reshape(KI, 128).T),
        bgate=np.ascontiguousarray(b_gate.reshape(KI, 128).T),
        lnw=np.ascontiguousarray(ln_w.reshape(KH, 128).T),
        lnb=np.ascontiguousarray(ln_b.reshape(KH, 128).T),
        dtbias=np.ascontiguousarray(dt_bias.reshape(8, 1)),
        alog=np.ascontiguousarray(A_log.reshape(8, 1)),
        tri=tri, sel=sel, selpair=selpair,
        i128=np.eye(128, dtype=f32),
        i16=np.eye(16, dtype=f32),
    )
    per_core = []
    for b in range(8):
        m = dict(shared)
        m["u_t"] = np.ascontiguousarray(u[b].T)
        per_core.append(m)
    return per_core


def build_nc(nch=NCH_FULL, dbg=()):
    nc = bass.Bass("TRN2", target_bir_lowering=False)

    def dram(name, shape, dtype, kind="ExternalInput"):
        return nc.dram_tensor(name, shape, dtype, kind=kind).ap()

    u_t = dram("u_t", [D, LCORE], F32)
    w_in = dram("w_in", [KD, 128, FW], F32)
    w_gate = dram("w_gate", [KI, 128, DI], F32)
    w_out = dram("w_out", [KH, 128, D], F32)
    w_fc = dram("w_fc", [KD, 128, 4096], F32)
    w_proj = dram("w_proj", [KH, 128, D], F32)
    convw_i = dram("convw", [128, 72], F32)
    convb_i = dram("convb", [128, 18], F32)
    dcol_i = dram("dcol", [128, KI], F32)
    gam_i = dram("gam", [128, KI], F32)
    bet_i = dram("bet", [128, KI], F32)
    bgate_i = dram("bgate", [128, KI], F32)
    lnw_i = dram("lnw", [128, KH], F32)
    lnb_i = dram("lnb", [128, KH], F32)
    dtbias_i = dram("dtbias", [8, 1], F32)
    alog_i = dram("alog", [8, 1], F32)
    tri_i = dram("tri", [128, 128], F32)
    sel_i = dram("sel", [8, 8 * 128], F32)
    selpair_i = dram("selpair", [8, 4 * 64], F32)
    i128_i = dram("i128", [128, 128], F32)
    i16_i = dram("i16", [16, 16], F32)

    out_t = dram("out_t", [D, LCORE], F32, kind="ExternalOutput")

    def sk(name):
        return "ExternalOutput" if name in dbg else "Internal"
    wq_fc = dram("wq_fc", [KD, 128, 4096], BF16, kind=sk("wq_fc"))
    wq_proj = dram("wq_proj", [KH, 128, D], BF16, kind=sk("wq_proj"))
    sp_yln = dram("sp_yln", [NCH_FULL, 128, KI * C], BF16, kind=sk("sp_yln"))
    sp_mlp = dram("sp_mlp", [NCH_FULL, 128, KI * C], BF16, kind=sk("sp_mlp"))
    sp_xr = dram("sp_xr", [NCH_FULL, 128, KD * C], F32, kind=sk("sp_xr"))
    dbg_x = dram("dbg_x", [NCH_FULL, 128, 16 * C], BF16, kind="ExternalOutput") if "dbg_x" in dbg else None
    dbg_bc = dram("dbg_bc", [NCH_FULL, 64, C], BF16, kind="ExternalOutput") if "dbg_bc" in dbg else None
    dbg_cc = dram("dbg_cc", [NCH_FULL, 48, C], BF16, kind="ExternalOutput") if "dbg_bc" in dbg else None
    dbg_dt = dram("dbg_dt", [NCH_FULL, 8, C], F32, kind="ExternalOutput") if "dbg_dt" in dbg else None
    dbg_y = dram("dbg_y", [NCH_FULL, 128, KI * C], BF16, kind="ExternalOutput") if "dbg_y" in dbg else None

    from contextlib import ExitStack
    with ExitStack() as _ctx:
        def _sb(name, shape, dtype):
            return _ctx.enter_context(nc.sbuf_tensor(name, shape, dtype))
        def _ps(name, shape):
            return _ctx.enter_context(nc.psum_tensor(name, shape, F32))
        def _sem(name):
            return _ctx.enter_context(nc.semaphore(name))
        wslab = _sb("wslab", [128, 57856], BF16)
        xn = _sb("xn", [128, KD * C], BF16)
        mlp = _sb("mlp", [128, KI * C], BF16)
        xbc = _sb("xbc", [128, 18 * CP3 + 2], BF16)
        xpost = _sb("xpost", [128, 16 * C], BF16)
        bpost = _sb("bpost", [16, C], BF16)
        cpost = _sb("cpost", [48, C], BF16)
        xt = _sb("xt", [128, DI], BF16)
        ybf = _sb("ybf", [128, KI * C], BF16)
        hbuf = _sb("hbuf", [128, KH * C], BF16)
        qmisc = _sb("qmisc", [128, 8], F32)
        dtraw = _sb("dtraw", [72, C], F32)
        dtv = _sb("dtv", [72, C], F32)
        trow = _sb("trow", [72, C], F32)
        trow2 = _sb("trow2", [72, C], F32)
        lrow = _sb("lrow", [72, Q], F32)
        srow = _sb("srow", [72, Q], F32)
        r8 = _sb("r8", [72, Q], F32)
        farg = _sb("farg", [72, Q], F32)
        ones8 = _sb("ones8", [72, Q], F32)
        sqc8 = _sb("sqc8", [72, 1], F32)
        negrT = _sb("negrT", [128, 8], F32)
        fcol = _sb("fcol", [128, 8], F32)
        btq = _sb("btq", [128, 16], BF16)
        btall = _sb("btall", [128, 4 * 64], BF16)
        ctil = _sb("ctil", [48, Q], BF16)
        m1f = _sb("m1f", [128, Q], F32)
        mh = _sb("mh", [128, Q], BF16)
        wth = _sb("wth", [128, Q], BF16)
        hp0 = _sb("hp0", [64, 512], BF16)
        hp1 = _sb("hp1", [64, 512], BF16)
        hp2 = _sb("hp2", [64, 512], BF16)
        hp3 = _sb("hp3", [64, 512], BF16)
        sq64 = _sb("sq64", [64, 4], F32)
        murow = _sb("murow", [1, C], BF16)
        rsrow = _sb("rsrow", [1, C], BF16)
        lntmp = _sb("lntmp", [1, 2 * C], F32)
        cw = _sb("cw_s", [128, 72], F32)
        cb = _sb("cb_s", [128, 18], F32)
        dcol = _sb("dcol_s", [128, KI], F32)
        gamc = _sb("gamc", [128, KI], F32)
        betc = _sb("betc", [128, KI], BF16)
        bgatec = _sb("bgatec", [128, KI], F32)
        bgeff = _sb("bgeff", [128, KI], F32)
        lnwc = _sb("lnwc", [128, KH], F32)
        lnbc = _sb("lnbc", [128, KH], BF16)
        bvecp = _sb("bvecp", [128, KD], F32)
        dtbias = _sb("dtbias_s", [72, 1], F32)
        acol = _sb("acol_s", [72, 1], F32)
        tri = _sb("tri_s", [128, 128], F32)
        sel = _sb("sel_s", [72, 8 * 128], F32)
        selpair = _sb("selpair_s", [72, 4 * 64], F32)
        i128 = _sb("i128_s", [128, 128], BF16)
        i16c = _sb("i16c", [16, 16], BF16)
        i8f = _sb("i8f", [72, 8], F32)
        fconst = _sb("fconst", [128, 4], F32)
        onecol = _sb("onecol", [128, 1], BF16)
        onecolf = _sb("onecolf", [128, 1], F32)
        onerow = _sb("onerow", [1, 128], BF16)
        onerowf = _sb("onerowf", [1, 128], F32)
        pb0 = _ps("pb0", [128, 512])
        pb1 = _ps("pb1", [128, 512])
        pb2 = _ps("pb2", [128, 512])
        pb3 = _ps("pb3", [128, 512])
        pb4 = _ps("pb4", [128, 512])
        pb5 = _ps("pb5", [128, 512])
        pb6 = _ps("pb6", [128, 512])
        pb7 = _ps("pb7", [128, 512])
        s_pe = _sem("s_pe")
        s_dv = _sem("s_dv")
        s_ac = _sem("s_ac")
        s_ld0 = _sem("s_ld0")
        s_ld1 = _sem("s_ld1")
        s_cast = _sem("s_cast")
        s_ub = _sem("s_ub")
        s_st = _sem("s_st")
        s_st1 = _sem("s_st1")
        s_stm = _sem("s_stm")
        s_sty = _sem("s_sty")
        s_stx = _sem("s_stx")
        s_sto = _sem("s_sto")
        block = _ctx.enter_context(nc.Block())

        qbuf0 = hbuf[:, 0:2048].bitcast(F32)
        qbuf1 = hbuf[:, 2048:4096].bitcast(F32)
        qout0 = hbuf[:, 4096:5120]
        qout1 = hbuf[:, 5120:6144]
        qm1v0 = ybf[:, 0:1024]
        qm1v1 = ybf[:, 1024:2048]
        qred = ybf[:, 2048:2176].bitcast(F32)
        sq_usq = hbuf[:, 4096:6144]
        sq_ysq = hbuf[:, 0:4096]
        sq_p2 = hbuf[:, 0:4096]
        yfin = hbuf[:, 4096:8192]
        sq_p3 = mlp[:, 0:2048]
        ubf = hbuf[:, 6144:8192]
        ut = xbc[:, 0:4096].bitcast(F32)
        dta = trow
        lndt = trow2

        def wout_sl(k, phase):
            if k < 24:
                return wslab[:, 32768 + k * 1024:32768 + (k + 1) * 1024]
            if k < 28:
                return xpost[:, (k - 24) * 1024:(k - 23) * 1024]
            if k < 30:
                return xt[:, (k - 28) * 1024:(k - 27) * 1024]
            if phase == 2:
                return xn[:, (k - 30) * 1024:(k - 29) * 1024]
            return mlp[:, 2048 + (k - 30) * 1024:2048 + (k - 29) * 1024]
        hp = [hp0, hp1, hp2, hp3]

        class Ctr:
            __slots__ = ("h", "n")

            def __init__(self, h):
                self.h = h
                self.n = 0

        sems = dict(pe=Ctr(s_pe), dv=Ctr(s_dv), ac=Ctr(s_ac),
                    ld0=Ctr(s_ld0), ld1=Ctr(s_ld1), cast=Ctr(s_cast),
                    ub=Ctr(s_ub), st=Ctr(s_st), st1=Ctr(s_st1), stm=Ctr(s_stm),
                    sty=Ctr(s_sty), stx=Ctr(s_stx), sto=Ctr(s_sto))
        ENG = dict(sync=nc.sync, gpsimd=nc.gpsimd, tensor=nc.tensor,
                   vector=nc.vector, scalar=nc.scalar)

        def program(me):
            for s in sems.values():
                s.n = 0
            E = ENG[me]
            rel = {}

            def op(eng, fn, sem=None, k=1):
                if sem is not None:
                    sems[sem].n += k
                if eng == me:
                    ins = fn()
                    if sem is not None:
                        ins.then_inc(sems[sem].h, k)

            def release(resname, *semnames):
                d = rel.setdefault(resname, {})
                for s in semnames:
                    d[s] = sems[s].n

            def acquire(eng, *resnames):
                if eng != me:
                    return
                need = {}
                for rn in resnames:
                    for s, v in rel.get(rn, {}).items():
                        need[s] = max(need.get(s, 0), v)
                for s, v in need.items():
                    if v > 0:
                        E.wait_ge(sems[s].h, v)

            def wtall(eng, sem):
                if eng == me:
                    E.wait_ge(sems[sem].h, sems[sem].n)

            T, V, A, G, SY = nc.tensor, nc.vector, nc.scalar, nc.gpsimd, nc.sync

            for dstap, srcap in ((cw[:, :], convw_i), (cb[:, :], convb_i), (dcol[:, :], dcol_i),
                                 (gamc[:, :], gam_i), (bgatec[:, :], bgate_i), (lnwc[:, :], lnw_i),
                                 (dtbias[64:72, :], dtbias_i), (acol[64:72, :], alog_i),
                                 (tri[:, :], tri_i), (sel[64:72, :], sel_i),
                                 (selpair[64:72, :], selpair_i), (i8f[64:72, :], i16_i[0:8, 0:8])):
                op("sync", lambda dstap=dstap, srcap=srcap: SY.dma_start(out=dstap, in_=srcap), "ld0", 16)
            for dst, src in ((betc, bet_i), (lnbc, lnb_i), (i128, i128_i), (i16c, i16_i)):
                op("gpsimd", lambda dst=dst, src=src: G.dma_start(out=dst[:, :], in_=src), "cast", 16)
            for k in range(KD):
                op("gpsimd", lambda k=k: G.dma_start(
                    out=wslab[:, k * FW:(k + 1) * FW], in_=w_in[k]), "cast", 16)
            release("winslab_ld", "cast")

            memset_list = [(fconst[:, 0:1], LN_EPS), (fconst[:, 1:2], 1.0),
                           (fconst[:, 2:3], 0.5 * QEPS),
                           (ones8[64:72, :], 1.0), (xbc[:, :], 0.0), (btall[:, :], 0.0),
                           (hp0[:, :], 0.0), (hp1[:, :], 0.0), (hp2[:, :], 0.0),
                           (hp3[:, :], 0.0), (onecol[:, :], 1.0), (onecolf[:, :], 1.0),
                           (onerow[:, :], 1.0), (onerowf[:, :], 1.0)]
            for idx, (t_, val) in enumerate(memset_list):
                last = idx == len(memset_list) - 1
                op("vector", lambda t_=t_, val=val: V.memset(t_, val),
                   "dv" if last else None)
            release("memsets", "dv")

            wtall("scalar", "ld0")
            op("scalar", lambda: A.activation(acol[64:72, :], acol[64:72, :], AF.Exp), "ac")
            wtall("vector", "ac")
            op("vector", lambda: V.tensor_scalar_mul(acol[64:72, :], acol[64:72, :], -1.0), "dv")
            release("acol", "dv")

            acquire("sync", "acol")
            qsrc = [(w_fc, KD, 4096, wq_fc), (w_proj, KH, D, wq_proj)]
            pieces_all = []
            for (wsrc, kc, fdim, wdst) in qsrc:
                p = []
                for k in range(kc):
                    for f0 in range(0, fdim, 1024):
                        p.append((k, f0, min(1024, fdim - f0)))
                pieces_all.append(p)

            for mi in range(2):
                wsrc, kc, fdim, wdst = qsrc[mi]
                for pi, (k, f0, fl) in enumerate(pieces_all[mi]):
                    par = pi % 2
                    buf = qbuf0 if par == 0 else qbuf1
                    lds = "ld0" if par == 0 else "ld1"
                    col = mi * 32 + pi
                    acquire("sync", f"qbuf{par}")
                    op("sync", lambda k=k, f0=f0, fl=fl, buf=buf: SY.dma_start(
                        out=buf[:, 0:fl], in_=wsrc[k][:, f0:f0 + fl]), lds, 16)
                    wtall("vector", lds)
                    op("vector", lambda buf=buf, fl=fl, col=col: V.tensor_reduce(
                        qred[:, col:col + 1], buf[:, 0:fl], AX.X, ALU.add,
                        apply_absolute_value=True), "dv")
                    release(f"qbuf{par}", "dv")
            release("qredA", "dv")
            for mi in range(2):
                wsrc, kc, fdim, wdst = qsrc[mi]
                acquire("vector", "qredA")
                op("vector", lambda mi=mi: V.tensor_reduce(
                    qmisc[:, mi:mi + 1], qred[:, mi * 32:mi * 32 + 32], AX.X, ALU.add), "dv")
                wtall("tensor", "dv")
                acquire("tensor", "pb2")
                op("tensor", lambda mi=mi: T.matmul(
                    pb2[0:1, mi:mi + 1], qmisc[:, mi:mi + 1], onecolf[:, :],
                    start=True, stop=True), "pe")
                nel = float(128 * kc * fdim)
                wtall("scalar", "pe")
                op("scalar", lambda mi=mi, nel=nel: A.activation(
                    qmisc[0:1, 2 + mi:3 + mi], pb2[0:1, mi:mi + 1], AF.Copy,
                    bias=0.5 * QEPS, scale=0.5 / nel), "ac")
                release("pb2", "ac")
                wtall("tensor", "ac")
                acquire("tensor", "pb3")
                op("tensor", lambda mi=mi: T.matmul(
                    pb3[:, mi:mi + 1], onerowf[:, :], qmisc[0:1, 2 + mi:3 + mi],
                    start=True, stop=True), "pe")
                wtall("vector", "pe")
                op("vector", lambda mi=mi: V.tensor_copy(
                    qmisc[:, 4 + mi:5 + mi], pb3[:, mi:mi + 1]), "dv")
                release("pb3", "dv")
                wtall("vector", "dv")
                op("vector", lambda mi=mi: V.tensor_scalar_mul(
                    qmisc[:, 6 + mi:7 + mi], qmisc[:, 4 + mi:5 + mi], -1.0), "dv")
            release("thr", "dv")
            release("pb2", "dv")
            release("pb3", "dv")

            for mi in range(2):
                wsrc, kc, fdim, wdst = qsrc[mi]
                for pi, (k, f0, fl) in enumerate(pieces_all[mi]):
                    par = pi % 2
                    buf = qbuf0 if par == 0 else qbuf1
                    obuf = qout0 if par == 0 else qout1
                    m1v = (qm1v0 if par == 0 else qm1v1)[:, 0:fl]
                    lds = "ld0" if par == 0 else "ld1"
                    acquire("sync", f"qbuf{par}", "thr")
                    op("sync", lambda k=k, f0=f0, fl=fl, buf=buf: SY.dma_start(
                        out=buf[:, 0:fl], in_=wsrc[k][:, f0:f0 + fl]), lds, 16)
                    wtall("vector", lds)
                    acquire("vector", f"qout{par}")
                    op("vector", lambda buf=buf, m1v=m1v, fl=fl, mi=mi: V.tensor_single_scalar(
                        m1v, buf[:, 0:fl], qmisc[:, 4 + mi:5 + mi], ALU.is_ge), None)
                    op("vector", lambda buf=buf, obuf=obuf, fl=fl, mi=mi: V.tensor_single_scalar(
                        obuf[:, 0:fl], buf[:, 0:fl], qmisc[:, 6 + mi:7 + mi], ALU.is_le), "dv")
                    release(f"qbuf{par}", "dv")
                    wtall("vector", "dv")
                    op("vector", lambda obuf=obuf, m1v=m1v, fl=fl: V.tensor_sub(
                        obuf[:, 0:fl], m1v, obuf[:, 0:fl]), "dv")
                    wtall("scalar", "dv")
                    stsem = "st" if par == 0 else "st1"
                    op("scalar", lambda wdst=wdst, k=k, f0=f0, fl=fl, obuf=obuf: A.dma_start(
                        out=wdst[k][:, f0:f0 + fl], in_=obuf[:, 0:fl]), stsem, 16)
                    release(f"qout{par}", stsem)
            release("ybf", "dv")
            release("quantdone", "st", "st1")
            release("squ", "dv", "st", "st1")
            release("hbufq", "dv", "st", "st1")

            ut3 = ut.rearrange("p (k t) -> p k t", t=C)
            ubf3 = ubf.rearrange("p (k t) -> p k t", t=C)
            xn3 = xn[:, :].rearrange("p (k t) -> p k t", t=C)
            u_src = u_t.rearrange("(k p) t -> p k t", p=128)
            xbc3 = xbc[:, 0:18 * CP3].rearrange("p (s t) -> p s t", t=CP3)

            def ln_rows(nfeat, tag):
                inv = 1.0 / nfeat
                acquire("vector", "lnrows")
                wtall("vector", "pe")
                op("vector", lambda: V.tensor_scalar_mul(murow[:, :], pb2[0:1, 0:C], inv), None)
                op("vector", lambda: V.tensor_scalar_mul(lntmp[:, 0:C], pb2[0:1, 0:C], inv), None)
                op("vector", lambda: V.tensor_scalar_mul(lntmp[:, C:2 * C], pb2[0:1, C:2 * C], inv), "dv")
                release("pb2", "dv")
                wtall("scalar", "dv")
                op("scalar", lambda: A.activation(trow[0:1, 0:C], lntmp[:, 0:C], AF.Square), "ac")
                wtall("vector", "ac")
                op("vector", lambda: V.tensor_sub(lntmp[:, C:2 * C], lntmp[:, C:2 * C], trow[0:1, 0:C]), "dv")
                wtall("scalar", "dv")
                op("scalar", lambda: A.activation(lntmp[:, 0:C], lntmp[:, C:2 * C], AF.Ln, bias=fconst[0:1, 0:1]), "ac")
                wtall("scalar", "ac")
                op("scalar", lambda: A.activation(rsrow[:, :], lntmp[:, 0:C], AF.Exp, scale=-0.5), "ac")
                wtall("tensor", "ac")
                acquire("tensor", "pb3")
                op("tensor", lambda: T.matmul(pb3[:, 0:C], onerow[:, :], murow[:, :], start=True, stop=False), None)
                op("tensor", lambda: T.matmul(pb3[:, C:2 * C], onerow[:, :], rsrow[:, :], start=False, stop=True), "pe")
                release("lnrows", "pe")

            for c in range(nch):
                c0 = c * C
                acquire("gpsimd", "ubf")
                op("gpsimd", lambda c0=c0: G.dma_start(
                    out=ubf3, in_=u_src[:, :, c0:c0 + C]), "ub", 16)

                wtall("scalar", "ub")
                acquire("scalar", "squ")
                op("scalar", lambda: A.activation(sq_usq, ubf, AF.Square), "ac")
                release("usq", "ac")
                if c == 0:
                    acquire("tensor", "winslab_ld", "memsets")
                wtall("tensor", "ub")
                acquire("tensor", "pb2")
                for k in range(KD):
                    op("tensor", lambda k=k: T.matmul(
                        pb2[0:1, 0:C], onecol[:, :], ubf3[:, k, :],
                        start=(k == 0), stop=False), None)
                wtall("tensor", "ac")
                for k in range(KD):
                    op("tensor", lambda k=k: T.matmul(
                        pb2[0:1, C:2 * C], onecol[:, :], sq_usq[:, k * C:(k + 1) * C],
                        start=False, stop=(k == KD - 1)),
                        "pe" if k == KD - 1 else None)
                release("squ", "pe")

                ln_rows(D, f"a{c}")
                wtall("vector", "pe")
                acquire("vector", "xn")
                op("vector", lambda: V.tensor_tensor(
                    xn3, ubf3, pb3[:, None, 0:C].broadcast_to([128, KD, C]), ALU.subtract), "dv")
                wtall("vector", "dv")
                op("vector", lambda: V.tensor_tensor(
                    xn3, xn3, pb3[:, None, C:2 * C].broadcast_to([128, KD, C]), ALU.mult), "dv")
                release("xnrdy", "dv")
                release("pb3", "dv")
                release("ubf", "pe", "ac", "dv")

                rounds = ([(16 + 2 * i, 17 + 2 * i, "x0") for i in range(8)]
                          + [(2 * i, 2 * i + 1, "z0") for i in range(8)]
                          + [(32 + 2 * i, 33 + 2 * i, "x") for i in range(8)]
                          + [(48, 49, "bcdt")])

                acquire("vector", "xbcw", "memsets")
                if c > 0:
                    op("vector", lambda: V.tensor_copy(
                        xbc3[:, :, 0:3], xbc3[:, :, C:C + 3]), "dv")
                release("tail", "dv")

                for r, (fa, fb, kind) in enumerate(rounds):
                    bank = pb0 if r % 2 == 0 else pb1
                    bname = f"pbw{r % 2}"
                    acquire("tensor", bname)
                    if r == 0:
                        wtall("tensor", "dv")
                    mm_n = 8 if fb is None else 16
                    cnt = 0
                    for fi, off in ((fa, 0), (fb, C)):
                        if fi is None:
                            continue
                        for k in range(KD):
                            cnt += 1
                            op("tensor", lambda fi=fi, off=off, k=k, bank=bank, s=(cnt == 1), cnt=cnt, mm_n=mm_n: T.matmul(
                                bank[:, off:off + C],
                                wslab[:, k * FW + fi * 128:k * FW + (fi + 1) * 128],
                                xn3[:, k, :], start=s, stop=(cnt == mm_n)),
                                "pe" if cnt == mm_n else None)
                    release(f"wimm{r}", "pe")
                    if kind == "x0":
                        i = (fa - 16) // 2
                        wtall("vector", "pe")
                        acquire("vector", "mlp")
                        op("vector", lambda i=i, bank=bank: V.tensor_copy(
                            mlp[:, (2 * i) * C:(2 * i + 2) * C], bank[:, 0:2 * C]), "dv")
                        release(bname, "dv")
                    elif kind == "z0":
                        i = fa // 2
                        wtall("scalar", "pe")
                        acquire("scalar", f"hz{i}", "hbufq")
                        op("scalar", lambda i=i, bank=bank: A.activation(
                            hbuf[:, (2 * i) * C:(2 * i + 2) * C], bank[:, 0:2 * C], AF.Silu), "ac")
                        release(bname, "ac")
                        wtall("vector", "ac")
                        op("vector", lambda i=i: V.tensor_tensor(
                            mlp[:, (2 * i) * C:(2 * i + 2) * C],
                            hbuf[:, (2 * i) * C:(2 * i + 2) * C],
                            mlp[:, (2 * i) * C:(2 * i + 2) * C], ALU.mult), "dv")
                        release(f"hz{i}", "dv")
                        if i == 7:
                            release("mlprdy", "dv")
                    elif kind == "x":
                        s0 = fa - 32
                        wtall("vector", "pe")
                        acquire("vector", "tail", "xbcw2", "memsets")
                        op("vector", lambda s0=s0, bank=bank: V.tensor_copy(
                            xbc3[:, s0:s0 + 2, 3:3 + C],
                            bank[:, 0:2 * C].rearrange("p (s t) -> p s t", t=C)), "dv")
                        release(bname, "dv")
                    else:
                        wtall("vector", "pe")
                        acquire("vector", "bcpost", "dtraw", "memsets")
                        op("vector", lambda bank=bank: V.tensor_copy(
                            xbc[0:16, 16 * CP3 + 3:16 * CP3 + 3 + C], bank[0:16, 0:C]), None)
                        op("vector", lambda bank=bank: V.tensor_copy(
                            dtraw[64:72, :], bank[64:72, 0:C]), None)
                        op("vector", lambda bank=bank: V.tensor_copy(
                            xbc[0:48, 17 * CP3 + 3:17 * CP3 + 3 + C], bank[0:48, C:2 * C]), "dv")
                        release(bname, "dv")
                release("winmm", "pe")
                release("xn", "pe")

                wtall("scalar", "dv")
                op("scalar", lambda c=c: A.dma_start(out=sp_mlp[c], in_=mlp[:, :]), "stm", 16)
                release("mlp", "stm")

                wtall("vector", "dv")
                acquire("vector", "xpost", "memsets")
                for j in range(4):
                    for s in range(18):
                        pr = 128 if s < 16 else (16 if s == 16 else 48)
                        win = xbc[0:pr, s * CP3 + j:s * CP3 + j + C]
                        if s < 16:
                            dst = xpost[:, s * C:(s + 1) * C]
                        elif s == 16:
                            dst = bpost[:, :]
                        else:
                            dst = cpost[:, :]
                        wv = cw[0:pr, 4 * s + j:4 * s + j + 1]
                        last = s == 17
                        if j == 0:
                            op("vector", lambda dst=dst, win=win, wv=wv: V.tensor_single_scalar(
                                dst, win, wv, ALU.mult), "dv" if last else None)
                        else:
                            op("vector", lambda dst=dst, win=win, wv=wv: V.scalar_tensor_tensor(
                                dst, win, wv, dst, ALU.mult, ALU.add), "dv" if last else None)
                    wtall("vector", "dv")
                release("xbcw", "dv")
                release("xbcw2", "dv")
                wtall("scalar", "dv")
                for s in range(18):
                    if s < 16:
                        op("scalar", lambda s=s: A.activation(
                            xpost[:, s * C:(s + 1) * C], xpost[:, s * C:(s + 1) * C],
                            AF.Silu, bias=cb[:, s:s + 1]), "ac" if s == 15 else None)
                    elif s == 16:
                        op("scalar", lambda: A.activation(
                            bpost[:, :], bpost[:, :], AF.Silu, bias=cb[0:16, 16:17]), "ac")
                    else:
                        op("scalar", lambda: A.activation(
                            cpost[:, :], cpost[:, :], AF.Silu, bias=cb[0:48, 17:18]), "ac")
                release("convdone", "ac")

                if dbg_x is not None:
                    wtall("scalar", "ac")
                    op("scalar", lambda c=c: A.dma_start(out=dbg_x[c], in_=xpost[:, :]), "st", 16)
                if dbg_bc is not None:
                    wtall("scalar", "ac")
                    op("scalar", lambda c=c: A.dma_start(out=dbg_bc[c][0:16, :], in_=bpost[:, :]), "st", 16)
                    op("scalar", lambda c=c: A.dma_start(out=dbg_cc[c], in_=cpost[:, :]), "st", 16)

                wtall("vector", "dv")
                acquire("vector", "dtrows")
                op("vector", lambda: V.tensor_scalar_add(dtv[64:72, :], dtraw[64:72, :], dtbias[64:72, 0:1]), "dv")
                release("dtraw", "dv")
                wtall("scalar", "dv")
                op("scalar", lambda: A.activation(trow[64:72, :], dtv[64:72, :], AF.Abs), "ac")
                wtall("scalar", "ac")
                op("scalar", lambda: A.activation(trow2[64:72, :], trow[64:72, :], AF.Exp, scale=-1.0), "ac")
                wtall("scalar", "ac")
                op("scalar", lambda: A.activation(trow[64:72, :], trow2[64:72, :], AF.Ln, bias=fconst[64:72, 1:2]), "ac")
                wtall("vector", "ac")
                op("vector", lambda: V.tensor_scalar_max(trow2[64:72, :], dtv[64:72, :], 0.0), "dv")
                wtall("vector", "dv")
                op("vector", lambda: V.tensor_add(dtv[64:72, :], trow[64:72, :], trow2[64:72, :]), "dv")
                if dbg_dt is not None:
                    wtall("scalar", "dv")
                    op("scalar", lambda c=c: A.dma_start(out=dbg_dt[c], in_=dtv[64:72, :]), "st", 16)
                wtall("vector", "dv")
                op("vector", lambda: V.tensor_single_scalar(dta[64:72, :], dtv[64:72, :], acol[64:72, 0:1], ALU.mult), "dv")
                wtall("scalar", "dv")
                op("scalar", lambda: A.activation(lndt[64:72, :], dtv[64:72, :], AF.Ln), "ac")
                release("dtrdy", "dv", "ac")

                for qi in range(2):
                    tq0 = qi * Q
                    wtall("vector", "ac")
                    acquire("vector", "rows", "memsets")
                    op("vector", lambda tq0=tq0: V.tensor_tensor_scan(
                        lrow[64:72, :], ones8[64:72, :], dta[64:72, tq0:tq0 + Q], 0.0, ALU.mult, ALU.add), "dv")
                    wtall("vector", "dv")
                    op("vector", lambda tq0=tq0: V.tensor_sub(r8[64:72, :], lrow[64:72, :], lndt[64:72, tq0:tq0 + Q]), None)
                    op("vector", lambda tq0=tq0: V.tensor_sub(farg[64:72, :], lndt[64:72, tq0:tq0 + Q], lrow[64:72, :]), "dv")
                    wtall("vector", "dv")
                    op("vector", lambda: V.tensor_scalar_add(farg[64:72, :], farg[64:72, :], lrow[64:72, Q - 1:Q]), "dv")
                    wtall("scalar", "dv")
                    op("scalar", lambda: A.activation(srow[64:72, :], lrow[64:72, :], AF.Exp), None)
                    op("scalar", lambda: A.activation(sqc8[64:72, :], lrow[64:72, Q - 1:Q], AF.Exp), "ac")
                    release("rowsrdy", "dv", "ac")

                    wtall("tensor", "ac")
                    for r in range(4):
                        bank = pb0 if r % 2 == 0 else pb1
                        bname = f"pbw{r % 2}"
                        acquire("tensor", bname, "xt")
                        for j in range(4):
                            fc_ = 4 * r + j
                            op("tensor", lambda fc_=fc_, j=j, bank=bank, tq0=tq0: T.matmul(
                                bank[:, j * 128:(j + 1) * 128],
                                xpost[:, fc_ * C + tq0:fc_ * C + tq0 + Q],
                                i128[:, :], start=(j == 0), stop=(j == 3)),
                                "pe" if j == 3 else None)
                        wtall("vector", "pe")
                        op("vector", lambda r=r, bank=bank: V.tensor_copy(
                            xt[:, r * 512:(r + 1) * 512], bank[:, 0:512]), "dv")
                        release(bname, "dv")
                    release("xtrdy", "dv")

                    wtall("tensor", "dv")
                    acquire("tensor", "pb4")
                    op("tensor", lambda tq0=tq0: T.matmul(
                        pb4[:, 0:128], bpost[:, tq0:tq0 + Q], cpost[0:16, tq0:tq0 + Q],
                        start=True, stop=True), None)
                    op("tensor", lambda tq0=tq0: T.matmul(
                        pb4[:, 128:144], bpost[:, tq0:tq0 + Q], i16c[:, :],
                        start=False, stop=False, skip_group_check=True), None)
                    op("tensor", lambda: T.matmul(
                        pb4[:, 152:160], r8[64:72, :], i8f[64:72, :], start=False, stop=False, skip_group_check=True), None)
                    op("tensor", lambda: T.matmul(
                        pb4[:, 160:168], farg[64:72, :], i8f[64:72, :], start=False, stop=False, skip_group_check=True), "pe")

                    wtall("vector", "pe")
                    acquire("vector", "scanvecs")
                    op("vector", lambda: V.tensor_scalar_mul(negrT[:, :], pb4[:, 152:160], -1.0), None)
                    op("vector", lambda: V.tensor_copy(btq[:, :], pb4[:, 128:144]), "dv")
                    wtall("scalar", "dv")
                    op("scalar", lambda: A.activation(fcol[:, :], pb4[:, 160:168], AF.Exp), "ac")
                    wtall("vector", "ac")
                    acquire("vector", "btall")
                    for i in range(4):
                        op("vector", lambda i=i: V.tensor_single_scalar(
                            btall[:, i * 64:i * 64 + 16], btq[:, :], fcol[:, 2 * i:2 * i + 1], ALU.mult), None)
                        op("vector", lambda i=i: V.tensor_single_scalar(
                            btall[:, i * 64 + 32:i * 64 + 48], btq[:, :], fcol[:, 2 * i + 1:2 * i + 2], ALU.mult),
                            "dv" if i == 3 else None)
                    release("btallrdy", "dv")

                    for h in range(NH):
                        i = h // 2
                        yb = pb6 if i % 2 == 0 else pb7
                        ybn = f"pby{i % 2}"
                        half = (h % 2) * 256
                        acquire("tensor", "pb5")
                        lr48 = slice(0, 16) if h % 2 == 0 else slice(32, 48)
                        op("tensor", lambda h=h: T.matmul(
                            pb5[:, 0:128], sel[64:72, h * 128:(h + 1) * 128], lrow[64:72, :],
                            start=True, stop=True), None)
                        last5 = h % 2 == 1
                        op("tensor", lambda h=h, lr48=lr48, last5=last5: T.matmul(
                            pb5[lr48, 128:256], sel[64:72, h * 128:h * 128 + 16], srow[64:72, :],
                            start=False, stop=False, skip_group_check=True), "pe" if last5 else None)
                        if h % 2 == 0:
                            op("tensor", lambda i=i: T.matmul(
                                pb5[0:64, 256:257], selpair[64:72, i * 64:(i + 1) * 64], sqc8[64:72, :],
                                start=False, stop=False, skip_group_check=True), "pe")
                        release(f"pb5mm", "pe")

                        wtall("vector", "pe")
                        acquire("vector", "maskb", "memsets")
                        op("vector", lambda: V.tensor_add(m1f[:, :], tri[:, :], pb5[:, 0:128]), "dv")
                        wtall("scalar", "dv")
                        op("scalar", lambda h=h: A.activation(
                            mh[:, :], m1f[:, :], AF.Exp, bias=negrT[:, h:h + 1]), "ac")
                        wtall("vector", "ac")
                        op("vector", lambda: V.tensor_mul(wth[:, :], mh[:, :], pb4[:, 0:128]), None)
                        lastm = "dv" if h % 2 == 1 else None
                        op("vector", lambda tq0=tq0, lr48=lr48: V.tensor_mul(
                            ctil[lr48, :], cpost[lr48, tq0:tq0 + Q], pb5[lr48, 128:256]), "dv")
                        if h % 2 == 0:
                            op("vector", lambda i=i: V.tensor_copy(
                                sq64[:, i:i + 1], pb5[0:64, 256:257]), "dv")
                        release("pb5", "dv")
                        release("maskrdy", "dv")

                        wtall("tensor", "dv")
                        if h % 2 == 0:
                            acquire("tensor", ybn)
                        for pc in range(2):
                            op("tensor", lambda h=h, pc=pc, yb=yb, half=half: T.matmul(
                                yb[:, half + pc * 128:half + (pc + 1) * 128],
                                xt[:, h * 256 + pc * 128:h * 256 + (pc + 1) * 128],
                                wth[:, :], start=(h % 2 == 0 and pc == 0), stop=False), None)
                            lrows = slice(0, 16) if h % 2 == 0 else slice(32, 48)
                            lcols = slice((h % 2) * 256 + pc * 128, (h % 2) * 256 + (pc + 1) * 128)
                            op("tensor", lambda pc=pc, yb=yb, half=half, i=i, lrows=lrows, lcols=lcols: T.matmul(
                                yb[:, half + pc * 128:half + (pc + 1) * 128],
                                hp[i][lrows, lcols], ctil[lrows, :],
                                start=False, stop=(h % 2 == 1 and pc == 1)), "pe" if pc == 1 else None)
                        release("ymm", "pe")
                        release("maskb", "pe")

                        if h % 2 == 1:
                            wtall("vector", "pe")
                            acquire("vector", "ybf")
                            for hh in (h - 1, h):
                                hhalf = (hh % 2) * 256
                                for pc in range(2):
                                    fc_ = 2 * hh + pc
                                    op("vector", lambda fc_=fc_, pc=pc, yb=yb, hhalf=hhalf, tq0=tq0: V.scalar_tensor_tensor(
                                        ybf[:, fc_ * C + tq0:fc_ * C + tq0 + Q],
                                        xpost[:, fc_ * C + tq0:fc_ * C + tq0 + Q],
                                        dcol[:, fc_:fc_ + 1],
                                        yb[:, hhalf + pc * 128:hhalf + (pc + 1) * 128],
                                        ALU.mult, ALU.add), "dv" if (hh == h and pc == 1) else None)
                            release(ybn, "dv")

                        if h % 2 == 1:
                            sb = pb0 if i % 2 == 0 else pb1
                            sbn = f"pbw{i % 2}"
                            wtall("tensor", "dv")
                            acquire("tensor", sbn)
                            op("tensor", lambda i=i, sb=sb: T.matmul(
                                sb[0:64, 0:512], btall[:, i * 64:(i + 1) * 64],
                                xt[:, i * 512:(i + 1) * 512], start=True, stop=True), "pe")
                            wtall("vector", "pe")
                            acquire("vector", "memsets")
                            op("vector", lambda i=i, sb=sb: V.scalar_tensor_tensor(
                                hp[i][:, :], hp[i][:, :], sq64[:, i:i + 1], sb[0:64, 0:512],
                                ALU.mult, ALU.add), "dv")
                            release(sbn, "dv")
                            if i == 3:
                                release("btall", "dv")
                                release("scanvecs", "dv", "ac")
                    release("rows", "dv", "pe", "ac")
                    release("pb4", "dv")
                    release("xt", "pe")
                release("xpost", "pe", "dv")
                release("bcpost", "pe", "dv")
                release("dtrows", "dv", "pe", "ac")
                release("yevdone", "dv")

                if dbg_y is not None:
                    wtall("scalar", "dv")
                    op("scalar", lambda c=c: A.dma_start(out=dbg_y[c], in_=ybf[:, :]), "st", 16)

                wtall("scalar", "dv")
                acquire("scalar", "squ")
                op("scalar", lambda: A.activation(sq_ysq, ybf[:, :], AF.Square), "ac")
                wtall("tensor", "dv")
                acquire("tensor", "pb2")
                for k in range(KI):
                    op("tensor", lambda k=k: T.matmul(
                        pb2[0:1, 0:C], onecol[:, :], ybf[:, k * C:(k + 1) * C],
                        start=(k == 0), stop=False), None)
                wtall("tensor", "ac")
                for k in range(KI):
                    op("tensor", lambda k=k: T.matmul(
                        pb2[0:1, C:2 * C], onecol[:, :], sq_ysq[:, k * C:(k + 1) * C],
                        start=False, stop=(k == KI - 1)),
                        "pe" if k == KI - 1 else None)
                release("squ", "pe")
                ln_rows(DI, f"b{c}")
                ybf3 = ybf[:, :].rearrange("p (k t) -> p k t", t=C)
                wtall("vector", "pe")
                wtall("vector", "dv")
                if dbg_y is not None:
                    wtall("vector", "st")
                op("vector", lambda ybf3=ybf3: V.tensor_tensor(
                    ybf3, ybf3, pb3[:, None, 0:C].broadcast_to([128, KI, C]), ALU.subtract), "dv")
                wtall("vector", "dv")
                for k in range(KI):
                    op("vector", lambda k=k: V.scalar_tensor_tensor(
                        ybf[:, k * C:(k + 1) * C], ybf[:, k * C:(k + 1) * C],
                        gamc[:, k:k + 1], pb3[:, C:2 * C], ALU.mult, ALU.mult),
                        "dv" if k == KI - 1 else None)
                release("pb3", "dv")

                wtall("scalar", "dv")
                op("scalar", lambda c=c: A.dma_start(out=sp_yln[c], in_=ybf[:, :]), "sty", 16)
                release("ybf", "sty")
            release("hbuf", "dv")
            release("p1done", "pe", "dv", "ac", "sty")

            wtall("gpsimd", "pe")
            wtall("gpsimd", "dv")
            wtall("gpsimd", "ac")
            wtall("gpsimd", "st")
            for k in range(KI):
                op("gpsimd", lambda k=k: G.dma_start(
                    out=wslab[:, k * DI:(k + 1) * DI], in_=w_gate[k]), "cast", 16)
            for k in range(KH):
                op("gpsimd", lambda k=k: G.dma_start(
                    out=wout_sl(k, 2), in_=w_out[k]), "cast", 16)
            release("p2slab", "cast")

            wtall("tensor", "cast")
            acquire("tensor", "pb2")
            cnt = 0
            for fo in range(KI):
                for k in range(KI):
                    cnt += 1
                    op("tensor", lambda fo=fo, k=k, s=(cnt == 1), cnt=cnt: T.matmul(
                        pb2[:, fo:fo + 1],
                        wslab[:, k * DI + fo * 128:k * DI + (fo + 1) * 128],
                        betc[:, k:k + 1],
                        start=s, stop=(cnt == KI * KI)),
                        "pe" if cnt == KI * KI else None)
            wtall("vector", "pe")
            op("vector", lambda: V.tensor_add(bgeff[:, :], bgatec[:, :], pb2[:, 0:KI]), "dv")
            release("pb2", "dv")
            release("bgeff", "dv")

            for c in range(nch):
                c0 = c * C
                lds = "ld0" if c % 2 == 0 else "ld1"
                acquire("sync", "ybf", "mlp", "ut", "p1done")
                op("sync", lambda c=c: SY.dma_start(out=ybf[:, :], in_=sp_yln[c]), lds, 16)
                op("sync", lambda c=c: SY.dma_start(out=mlp[:, :], in_=sp_mlp[c]), lds, 16)
                op("sync", lambda c0=c0: SY.dma_start(out=ut3, in_=u_src[:, :, c0:c0 + C]), lds, 16)

                for r in range(8):
                    bank = pb0 if r % 2 == 0 else pb1
                    bname = f"pbw{r % 2}"
                    wtall("tensor", lds)
                    if c == 0 and r == 0:
                        wtall("tensor", "dv")
                    acquire("tensor", bname)
                    cnt = 0
                    for off, fi in ((0, 2 * r), (C, 2 * r + 1)):
                        for k in range(KI):
                            cnt += 1
                            op("tensor", lambda fi=fi, off=off, k=k, bank=bank, s=(cnt == 1), cnt=cnt: T.matmul(
                                bank[:, off:off + C],
                                wslab[:, k * DI + fi * 128:k * DI + (fi + 1) * 128],
                                ybf[:, k * C:(k + 1) * C],
                                start=s, stop=(cnt == 32)),
                                "pe" if cnt == 32 else None)
                    wtall("scalar", "pe")
                    acquire("scalar", "squ")
                    for off, fi in ((0, 2 * r), (C, 2 * r + 1)):
                        op("scalar", lambda fi=fi, off=off, bank=bank: A.activation(
                            sq_p2[:, fi * C:(fi + 1) * C], bank[:, off:off + C],
                            AF.Sigmoid, bias=bgeff[:, fi:fi + 1]),
                            "ac" if off == C else None)
                    release(bname, "ac")
                    wtall("vector", "ac")
                    for fi in (2 * r, 2 * r + 1):
                        op("vector", lambda fi=fi: V.scalar_tensor_tensor(
                            yfin[:, fi * C:(fi + 1) * C], ybf[:, fi * C:(fi + 1) * C],
                            betc[:, fi:fi + 1], sq_p2[:, fi * C:(fi + 1) * C],
                            ALU.add, ALU.mult), "dv" if fi % 2 == 1 else None)
                    release("squ", "dv")

                for r in range(4):
                    bank = pb0 if r % 2 == 0 else pb1
                    bname = f"pbw{r % 2}"
                    wtall("tensor", "dv")
                    acquire("tensor", bname)
                    cnt = 0
                    for off, do in ((0, 2 * r), (C, 2 * r + 1)):
                        for k in range(KH):
                            cnt += 1
                            src = mlp[:, k * C:(k + 1) * C] if k < KI else yfin[:, (k - KI) * C:(k - KI + 1) * C]
                            op("tensor", lambda do=do, off=off, k=k, bank=bank, s=(cnt == 1), cnt=cnt, src=src: T.matmul(
                                bank[:, off:off + C],
                                wout_sl(k, 2)[:, do * 128:(do + 1) * 128],
                                src, start=s, stop=(cnt == 64)),
                                "pe" if cnt == 64 else None)
                    wtall("vector", "pe")
                    wtall("vector", lds)
                    for off, do in ((0, 2 * r), (C, 2 * r + 1)):
                        op("vector", lambda do=do, off=off, bank=bank: V.tensor_add(
                            ut3[:, do, :], ut3[:, do, :], bank[:, off:off + C]),
                            "dv" if off == C else None)
                    release(bname, "dv")

                wtall("scalar", "dv")
                op("scalar", lambda c=c: A.dma_start(out=sp_xr[c], in_=ut), "stx", 16)
                release("ybf", "pe", "dv")
                release("mlp", "pe")
                release("ut", "stx")
            release("p2done", "pe", "dv", "stx")

            acquire("sync", "p2done", "quantdone")
            wtall("sync", "pe")
            wtall("sync", "dv")
            wtall("sync", "ac")
            for k in range(KD):
                op("sync", lambda k=k: SY.dma_start(
                    out=wslab[:, k * 4096:(k + 1) * 4096], in_=wq_fc[k]), "ld0", 16)
            for k in range(KH):
                op("sync", lambda k=k: SY.dma_start(
                    out=wout_sl(k, 3), in_=wq_proj[k]), "ld0", 16)

            wtall("tensor", "ld0")
            acquire("tensor", "pb2")
            cnt = 0
            for do in range(KD):
                for k in range(KH):
                    cnt += 1
                    op("tensor", lambda do=do, k=k, s=(cnt == 1), cnt=cnt: T.matmul(
                        pb2[:, do:do + 1],
                        wout_sl(k, 3)[:, do * 128:(do + 1) * 128],
                        lnbc[:, k:k + 1],
                        start=s, stop=(cnt == KD * KH)),
                        "pe" if cnt == KD * KH else None)
            wtall("vector", "pe")
            op("vector", lambda: V.tensor_copy(bvecp[:, :], pb2[:, 0:KD]), "dv")
            release("pb2", "dv")
            release("bvec", "dv")

            for c in range(nch):
                c0 = c * C
                lds = "ld0" if c % 2 == 0 else "ld1"
                acquire("sync", "ut")
                if c == 0:
                    acquire("sync", "bvec")
                op("sync", lambda c=c: SY.dma_start(out=ut, in_=sp_xr[c]), lds, 16)

                wtall("scalar", lds)
                acquire("scalar", "xn", "mlp", "p2done")
                op("scalar", lambda: A.activation(xn[:, :], ut, AF.Copy), "ac")
                wtall("scalar", "ac")
                op("scalar", lambda: A.activation(sq_p3[:, :], xn[:, :], AF.Square), "ac")
                wtall("tensor", "ac")
                acquire("tensor", "pb2")
                if c == 0:
                    wtall("tensor", "dv")
                for k in range(KD):
                    op("tensor", lambda k=k: T.matmul(
                        pb2[0:1, 0:C], onecol[:, :], xn3[:, k, :],
                        start=(k == 0), stop=False), None)
                for k in range(KD):
                    op("tensor", lambda k=k: T.matmul(
                        pb2[0:1, C:2 * C], onecol[:, :], sq_p3[:, k * C:(k + 1) * C],
                        start=False, stop=(k == KD - 1)),
                        "pe" if k == KD - 1 else None)
                release("mlp", "pe")
                ln_rows(D, f"c{c}")
                wtall("vector", "pe")
                wtall("vector", lds)
                op("vector", lambda: V.tensor_tensor(
                    xn3, ut3, pb3[:, None, 0:C].broadcast_to([128, KD, C]), ALU.subtract), "dv")
                wtall("vector", "dv")
                op("vector", lambda: V.tensor_tensor(
                    xn3, xn3, pb3[:, None, C:2 * C].broadcast_to([128, KD, C]), ALU.mult), "dv")
                release("pb3", "dv")

                for r in range(16):
                    bank = pb0 if r % 2 == 0 else pb1
                    bname = f"pbw{r % 2}"
                    wtall("tensor", "dv")
                    acquire("tensor", bname)
                    cnt = 0
                    for off, fi in ((0, 2 * r), (C, 2 * r + 1)):
                        for k in range(KD):
                            cnt += 1
                            op("tensor", lambda fi=fi, off=off, k=k, bank=bank, s=(cnt == 1), cnt=cnt: T.matmul(
                                bank[:, off:off + C],
                                wslab[:, k * 4096 + fi * 128:k * 4096 + (fi + 1) * 128],
                                xn3[:, k, :], start=s, stop=(cnt == 16)),
                                "pe" if cnt == 16 else None)
                    wtall("scalar", "pe")
                    acquire("scalar", "hbuf")
                    op("scalar", lambda r=r, bank=bank: A.activation(
                        hbuf[:, (2 * r) * C:(2 * r + 2) * C], bank[:, 0:2 * C], AF.Gelu), "ac")
                    release(bname, "ac")
                release("xn", "pe")
                release("hrdy", "ac")

                wtall("tensor", "ac")
                acquire("tensor", "pb2")
                for k in range(KH):
                    op("tensor", lambda k=k: T.matmul(
                        pb2[0:1, 0:C], onecol[:, :], hbuf[:, k * C:(k + 1) * C],
                        start=(k == 0), stop=False),
                        "pe" if k == KH - 1 else None)
                for g in range(8):
                    acquire("scalar", "mlp")
                    op("scalar", lambda g=g: A.activation(
                        sq_p3[:, (g % 2) * 1024:(g % 2) * 1024 + 1024],
                        hbuf[:, g * 1024:(g + 1) * 1024], AF.Square), "ac")
                    wtall("tensor", "ac")
                    for kk in range(4):
                        k = g * 4 + kk
                        op("tensor", lambda g=g, kk=kk, k=k: T.matmul(
                            pb2[0:1, C:2 * C], onecol[:, :],
                            sq_p3[:, (g % 2) * 1024 + kk * C:(g % 2) * 1024 + (kk + 1) * C],
                            start=False, stop=(k == KH - 1)),
                            "pe" if kk == 3 else None)
                    release("mlp", "pe")
                ln_rows(4096, f"d{c}")
                hb3 = hbuf[:, :].rearrange("p (k t) -> p k t", t=C)
                wtall("vector", "pe")
                op("vector", lambda hb3=hb3: V.tensor_tensor(
                    hb3, hb3, pb3[:, None, 0:C].broadcast_to([128, KH, C]), ALU.subtract), "dv")
                wtall("vector", "dv")
                for k in range(KH):
                    op("vector", lambda k=k: V.scalar_tensor_tensor(
                        hbuf[:, k * C:(k + 1) * C], hbuf[:, k * C:(k + 1) * C],
                        lnwc[:, k:k + 1], pb3[:, C:2 * C], ALU.mult, ALU.mult),
                        "dv" if k == KH - 1 else None)
                release("pb3", "dv")

                for r in range(4):
                    bank = pb0 if r % 2 == 0 else pb1
                    bname = f"pbw{r % 2}"
                    wtall("tensor", "dv")
                    acquire("tensor", bname)
                    cnt = 0
                    for off, do in ((0, 2 * r), (C, 2 * r + 1)):
                        for k in range(KH):
                            cnt += 1
                            op("tensor", lambda do=do, off=off, k=k, bank=bank, s=(cnt == 1), cnt=cnt: T.matmul(
                                bank[:, off:off + C],
                                wout_sl(k, 3)[:, do * 128:(do + 1) * 128],
                                hbuf[:, k * C:(k + 1) * C],
                                start=s, stop=(cnt == 64)),
                                "pe" if cnt == 64 else None)
                    wtall("vector", "pe")
                    for off, do in ((0, 2 * r), (C, 2 * r + 1)):
                        op("vector", lambda do=do, off=off, bank=bank: V.scalar_tensor_tensor(
                            ut3[:, do, :], bank[:, off:off + C], bvecp[:, do:do + 1],
                            ut3[:, do, :], ALU.add, ALU.add),
                            "dv" if off == C else None)
                    release(bname, "dv")
                release("hbuf", "pe")

                wtall("scalar", "dv")
                op("scalar", lambda c0=c0: A.dma_start(
                    out=out_t.rearrange("(k p) t -> p k t", p=128)[:, :, c0:c0 + C],
                    in_=ut3), "sto", 16)
                release("ut", "sto")

            wtall("scalar", "st")
            wtall("scalar", "st1")
            wtall("scalar", "stm")
            wtall("scalar", "sty")
            wtall("scalar", "stx")
            wtall("scalar", "sto")

        for name in ("sync", "gpsimd", "tensor", "vector", "scalar"):
            getattr(block, name)(lambda eng, name=name: program(name))

    return nc


def kernel(**inputs):
    per_core = _host_prep(inputs)
    nc = build_nc()
    res = run_bass_kernel_spmd(nc, per_core, core_ids=list(range(8)))
    outs = [np.ascontiguousarray(np.asarray(r["out_t"]).T) for r in res.results]
    return np.stack(outs, axis=0)
